# version 02 (final)
# baseline (speedup 1.0000x reference)
import contextlib
import numpy as np
import concourse.bass as bass
import concourse.mybir as mybir
from concourse.bass_utils import run_bass_kernel_spmd

F32 = mybir.dt.float32
BF16 = mybir.dt.bfloat16
I8 = mybir.dt.int8
AF = mybir.ActivationFunctionType
ALU = mybir.AluOpType
AX = mybir.AxisListType.X

NCORES = 8
T = 1024
NT = T // 128
D = 4096
O = 11008
NU = O // 128
NQ = 4
EPS = 1e-5
QSCALE = float(1.0 / np.sqrt(128.0))

ENGS = ("sync", "gpsimd", "tensor", "vector", "scalar")
PHASES = 4
NT1 = NT
SUB = 2
MVTEST = 0
SDTEST = 0
DBG = 0
NT3 = NT
SUB3 = 9


class Ctr:
    __slots__ = ("sem", "step", "v")

    def __init__(self, sem, step):
        self.sem, self.step, self.v = sem, step, 0

    def nxt(self):
        self.v += self.step
        return self.v


class Prog:
    def __init__(self):
        self.q = {e: [] for e in ENGS}

    def w(self, eng, ctr, val):
        if val > 0:
            s = ctr.sem if isinstance(ctr, Ctr) else ctr
            self.q[eng].append(lambda E, s=s, v=val: E.wait_ge(s, v))

    def do(self, eng, fn):
        self.q[eng].append(fn)

    def inc(self, eng, fn, ctr):
        v = ctr.nxt()
        self.q[eng].append(lambda E, f=fn, c=ctr: f(E).then_inc(c.sem, c.step))
        return v

    def nopinc(self, eng, ctr):
        v = ctr.nxt()
        self.q[eng].append(lambda E, c=ctr: E.nop().then_inc(c.sem, c.step))
        return v


def build_nc():
    nc = bass.Bass("TRN2", target_bir_lowering=False)
    ap = lambda t: t.ap()
    x_d = ap(nc.dram_tensor("x", [T, D], F32, kind="ExternalInput"))
    wqb_d = ap(nc.dram_tensor("wqb", [8, 128, 32, 4, 128], F32, kind="ExternalInput"))
    bq_d = ap(nc.dram_tensor("bq", [1, 4096], F32, kind="ExternalInput"))
    wkb_d = ap(nc.dram_tensor("wkb", [8, 128, 32, 128], F32, kind="ExternalInput"))
    bk_d = ap(nc.dram_tensor("bk", [1, 1024], F32, kind="ExternalInput"))
    wvb_d = ap(nc.dram_tensor("wvb", [8, 128, 32, 128], F32, kind="ExternalInput"))
    bv_d = ap(nc.dram_tensor("bv", [1, 1024], F32, kind="ExternalInput"))
    wgb_d = ap(nc.dram_tensor("wgb", [NU, 128, 32, 128], I8, kind="ExternalInput"))
    wub_d = ap(nc.dram_tensor("wub", [NU, 128, 32, 128], I8, kind="ExternalInput"))
    wdb_d = ap(nc.dram_tensor("wdb", [NU, 128, 4096], I8, kind="ExternalInput"))
    sg_d = ap(nc.dram_tensor("sg", [128, NU], F32, kind="ExternalInput"))
    su_d = ap(nc.dram_tensor("su", [128, NU], F32, kind="ExternalInput"))
    sd_d = ap(nc.dram_tensor("sd", [1, 4096], F32, kind="ExternalInput"))
    out_d = ap(nc.dram_tensor("out", [T, D], F32, kind="ExternalOutput"))
    dbg_d = ap(nc.dram_tensor("dbg", [128, 32 * T], F32, kind="ExternalOutput")) if DBG else None
    x2s_d = ap(nc.dram_tensor("x2s", [T, D], F32, kind="Internal"))

    P = Prog()
    with contextlib.ExitStack() as es:
        sb = lambda n, s, d: es.enter_context(nc.sbuf_tensor(n, s, d))
        psum = lambda n, s, d: es.enter_context(nc.psum_tensor(n, s, d))
        sem = lambda n: es.enter_context(nc.semaphore(n))

        cD = Ctr(sem("cD"), 16)
        cW = Ctr(sem("cW"), 16)
        cT = Ctr(sem("cT"), 1)
        cV = Ctr(sem("cV"), 1)
        cS = Ctr(sem("cS"), 1)

        pb = [psum(f"pb{i}", [128, 512], F32) for i in range(8)]
        pbf = pb[6].bitcast(BF16)

        ident = sb("ident", [128, 128], BF16)
        identf = sb("identf", [128, 128], F32)
        sg_s = sb("sg_s", [128, NU], F32)
        su_s = sb("su_s", [128, NU], F32)
        stat = sb("stat", [128, 144], F32)
        xb = [sb(f"xb{i}", [128, D], F32) for i in range(2)]
        lnt = [sb(f"lnt{i}", [128, D], BF16) for i in range(2)]

        P.inc("gpsimd", lambda g: g.dma_start(out=sg_s[:], in_=sg_d[:, :]), cW)
        P.inc("gpsimd", lambda g: g.dma_start(out=su_s[:], in_=su_d[:, :]), cW)
        P.do("gpsimd", lambda g: g.memset(ident[:], 1.0))
        P.do("gpsimd", lambda g: g.affine_select(ident[:], ident[:], [[-1, 128]],
                                                 ALU.is_equal, 0.0, base=0,
                                                 channel_multiplier=1))
        P.do("gpsimd", lambda g: g.memset(identf[:], 1.0))
        P.do("gpsimd", lambda g: g.affine_select(identf[:], identf[:], [[-1, 128]],
                                                 ALU.is_equal, 0.0, base=0,
                                                 channel_multiplier=1))
        P.do("gpsimd", lambda g: g.memset(stat[:, 140:141], EPS))
        v_ident = P.nopinc("gpsimd", cW)

        def emit_ln(in_ap, width, sq_out_ap, out_ap, s0, v_in_ctr, v_in):
            sl_ = lambda i: stat[:, s0 + 2 * i: s0 + 2 * i + 1]
            ssq, sm, mb, e2, msq, vr, sd2, rb = (sl_(i) for i in range(8))
            nm = stat[:, s0 + 1:s0 + 2]
            P.w("scalar", v_in_ctr, v_in)
            P.inc("scalar", lambda s, o=sq_out_ap, i=in_ap, a=ssq:
                  s.activation(o, i, AF.Square, accum_out=a), cS)
            v_sums = P.inc("scalar", lambda s, o=sq_out_ap, i=in_ap, a=sm:
                           s.activation(o, i, AF.Identity, accum_out=a), cS)
            P.w("vector", cS, v_sums)
            P.do("vector", lambda v, o=mb, i=sm, k=1.0 / width:
                 v.tensor_scalar_mul(o, i, k))
            v_mb = P.inc("vector", lambda v, o=e2, i=ssq, k=1.0 / width:
                         v.tensor_scalar_mul(o, i, k), cV)
            P.w("scalar", cV, v_mb)
            v_msq = P.inc("scalar", lambda s, o=msq, i=mb:
                          s.activation(o, i, AF.Square), cS)
            P.w("vector", cS, v_msq)
            v_var = P.inc("vector", lambda v, o=vr, a=e2, b=msq:
                          v.tensor_sub(o, a, b), cV)
            P.w("scalar", cV, v_var)
            P.w("scalar", cW, v_ident)
            v_sd2 = P.inc("scalar", lambda s, o=sd2, i=vr:
                          s.activation(o, i, AF.Sqrt, bias=stat[:, 140:141]), cS)
            P.w("vector", cS, v_sd2)
            v_rb = P.inc("vector", lambda v, o=rb, i=sd2: v.reciprocal(o, i), cV)
            P.w("scalar", cV, v_rb)
            v_nm = P.inc("scalar", lambda s, o=nm, i=mb, r=rb:
                         s.mul(o, i, r), cS)
            P.w("vector", cS, v_nm)
            v_nm2 = P.inc("vector", lambda v, o=nm, i=nm:
                          v.tensor_scalar_mul(o, i, -1.0), cV)
            P.w("scalar", cV, v_nm2)
            v_ap2 = P.inc("scalar", lambda s, o=out_ap, i=in_ap, b=nm, r=rb:
                          s.activation(o, i, AF.Identity, bias=b, scale=r), cS)
            return v_ap2

        xb_free = [(cS, 0), (cS, 0)]
        v_last_spill = [0] * NT

        with contextlib.ExitStack() as esA:
            qt = esA.enter_context(nc.sbuf_tensor("qt", [128, 3 * 8 * T], BF16))

            with contextlib.ExitStack() as esA1:
                sbA1 = lambda n, s, d: esA1.enter_context(nc.sbuf_tensor(n, s, d))
                ln1t = sbA1("ln1t", [128, 32 * T], BF16)
                wr = [sbA1(f"wr{i}", [128, 4096], BF16) for i in range(2)]
                rows = sbA1("rows", [1, 6656], BF16)
                bq_r = rows[0:1, 0:4096]
                bk_r = rows[0:1, 4096:5120]
                bv_r = rows[0:1, 5120:6144]
                ones_r = rows[0:1, 6144:6656]
                P.inc("gpsimd", lambda g: g.dma_start(out=bq_r, in_=bq_d[:, :]), cW)
                P.inc("gpsimd", lambda g: g.dma_start(out=bk_r, in_=bk_d[:, :]), cW)
                P.inc("gpsimd", lambda g: g.dma_start(out=bv_r, in_=bv_d[:, :]), cW)
                P.do("gpsimd", lambda g: g.memset(ones_r, 1.0))
                v_rows = P.nopinc("gpsimd", cW)

                lnt_free = [(cV, 0), (cV, 0)]
                pbf_free = (cV, 0)
                v_ln1_done = 0
                for t in range(NT1):
                    sl = t % 2
                    P.w("sync", *xb_free[sl])
                    v_x = P.inc("sync", lambda sy, o=xb[sl][:],
                                i=x_d[t * 128:(t + 1) * 128, :]:
                                sy.dma_start(out=o, in_=i), cD)
                    P.w("scalar", *lnt_free[sl])
                    v_ap2 = emit_ln(xb[sl][:, 0:D], D, lnt[sl][:, 0:D],
                                    lnt[sl][:, 0:D], 16 * sl, cD, v_x)
                    xb_free[sl] = (cS, v_ap2)
                    for b in range(4 if SUB >= 1 else 0):
                        P.w("tensor", cS, v_ap2)
                        P.w("tensor", *pbf_free)
                        if t == 0 and b == 0:
                            P.w("tensor", cW, v_ident)
                        v_tp = 0
                        for j in range(8):
                            c = b * 8 + j
                            fn = (lambda tt, o=pbf[:, j * 128:(j + 1) * 128],
                                  i=lnt[sl][:, c * 128:(c + 1) * 128], d=ident[:]:
                                  tt.transpose(o, i, d))
                            if j < 7:
                                P.do("tensor", fn)
                            else:
                                v_tp = P.inc("tensor", fn, cT)
                        if SUB >= 2:
                            P.w("vector", cT, v_tp)
                            dst = ln1t.rearrange("p (c s) -> p c s", s=T)[
                                :, b * 8:(b + 1) * 8, t * 128:(t + 1) * 128]
                            src = pbf.rearrange("p (c i) -> p c i", i=128)
                            v_ev = P.inc("vector", lambda v, o=dst, i=src:
                                         v.tensor_copy(o, i), cV)
                            pbf_free = (cV, v_ev)
                            if b == 3:
                                lnt_free[sl] = (cV, v_ev)
                                v_ln1_done = v_ev
                        else:
                            v_ev = P.nopinc("vector", cV)
                            pbf_free = (cV, v_ev)
                            if b == 3:
                                lnt_free[sl] = (cV, v_ev)
                                v_ln1_done = v_ev

                wr_free = [(cT, 0), (cT, 0)]
                pq_free = [(cS, 0)] * 4
                v_qkv_done = 0
                for g in range(8 if PHASES >= 2 else 0):
                    bs = 2 * (g % 2)
                    for qc in range(4):
                        h = qc % 2
                        P.w("gpsimd", *wr_free[h])
                        v_wq = P.inc("gpsimd",
                                     lambda gg, o=wr[h][:],
                                     i=wqb_d[g, :, 8 * qc:8 * (qc + 1), :, :]:
                                     gg.dma_start(out=o, in_=i), cW)
                        if qc == 0:
                            P.w("tensor", cV, v_ln1_done)
                            P.w("tensor", cW, v_rows)
                            P.w("tensor", *pq_free[bs])
                            P.w("tensor", *pq_free[bs + 1])
                        P.w("tensor", cW, v_wq)
                        v_qmm = 0
                        for c8 in range(8):
                            c = qc * 8 + c8
                            for j in range(4):
                                st_ap = wr[h][:, (c8 * 4 + j) * 128:
                                              (c8 * 4 + j + 1) * 128]
                                for sgi in range(2):
                                    fn = (lambda tt, o=pb[bs + sgi][:, 0:512],
                                          a=st_ap,
                                          b=ln1t[:, c * T + sgi * 512:
                                                 c * T + sgi * 512 + 512],
                                          s0=(c == 0 and j == 0):
                                          tt.matmul(o, a, b, start=s0, stop=False))
                                    if c8 == 7 and j == 3 and sgi == 1:
                                        v_qmm = P.inc("tensor", fn, cT)
                                    else:
                                        P.do("tensor", fn)
                        wr_free[h] = (cT, v_qmm)
                    v_mm = [0, 0]
                    for sgi in range(2):
                        for j in range(4):
                            fn = (lambda tt, o=pb[bs + sgi][:, 0:512],
                                  a=bq_r[0:1, 4 * g + j::32],
                                  b=ones_r[0:1, 0:512], sp=(j == 3):
                                  tt.matmul(o, a, b, start=False, stop=sp))
                            if j < 3:
                                P.do("tensor", fn)
                            else:
                                v_mm[sgi] = P.inc("tensor", fn, cT)
                    for sgi in range(2):
                        P.w("scalar", cT, v_mm[sgi])
                        dst = qt[:, g * T + sgi * 512: g * T + sgi * 512 + 512]
                        v_ev = P.inc("scalar",
                                     lambda s, o=dst, i=pb[bs + sgi][:, 0:512]:
                                     s.activation(o, i, AF.Identity, scale=QSCALE),
                                     cS)
                        pq_free[bs + sgi] = (cS, v_ev)
                for m, (wsrc, brow) in enumerate([(wkb_d, bk_r), (wvb_d, bv_r)] if PHASES >= 2 else []):
                    for g in range(8):
                        h = g % 2
                        bs = 2 * (g % 2)
                        P.w("gpsimd", *wr_free[h])
                        v_wk = P.inc("gpsimd", lambda gg, o=wr[h][:],
                                     i=wsrc[g, :, :, :]:
                                     gg.dma_start(out=o, in_=i), cW)
                        P.w("tensor", cW, v_wk)
                        P.w("tensor", *pq_free[bs])
                        P.w("tensor", *pq_free[bs + 1])
                        for c in range(32):
                            st_ap = wr[h][:, c * 128:(c + 1) * 128]
                            for sgi in range(2):
                                P.do("tensor",
                                     (lambda tt, o=pb[bs + sgi][:, 0:512], a=st_ap,
                                      b=ln1t[:, c * T + sgi * 512:
                                             c * T + sgi * 512 + 512],
                                      s0=(c == 0):
                                      tt.matmul(o, a, b, start=s0, stop=False)))
                        v_mm = [0, 0]
                        for sgi in range(2):
                            fn = (lambda tt, o=pb[bs + sgi][:, 0:512],
                                  a=brow[0:1, g::8], b=ones_r[0:1, 0:512]:
                                  tt.matmul(o, a, b, start=False, stop=True))
                            v_mm[sgi] = P.inc("tensor", fn, cT)
                        wr_free[h] = (cT, v_mm[1])
                        for sgi in range(2):
                            P.w("scalar", cT, v_mm[sgi])
                            dst = qt[:, (m + 1) * 8 * T + g * T + sgi * 512:
                                     (m + 1) * 8 * T + g * T + sgi * 512 + 512]
                            v_ev = P.inc("scalar",
                                         lambda s, o=dst, i=pb[bs + sgi][:, 0:512]:
                                         s.activation(o, i, AF.Identity, scale=1.0),
                                         cS)
                            pq_free[bs + sgi] = (cS, v_ev)
                            v_qkv_done = v_ev
                if DBG == 1:
                    P.w("gpsimd", cS, v_qkv_done)
                    P.inc("gpsimd", lambda g: g.dma_start(out=dbg_d[:, 0:24576], in_=qt[:, :]), cW)
                if DBG == 4:
                    P.w("gpsimd", cV, v_ln1_done)
                    P.inc("gpsimd", lambda g: g.dma_start(out=dbg_d[:, 0:144],
                                                          in_=stat[:, :]), cW)
                if DBG == 3:
                    P.w("gpsimd", cV, v_ln1_done)
                    P.inc("gpsimd", lambda g: g.dma_start(out=dbg_d[:, 0:D],
                                                          in_=lnt[0][:, :]), cW)
                if DBG == 2:
                    P.w("gpsimd", cV, v_ln1_done)
                    P.inc("gpsimd", lambda g: g.dma_start(out=dbg_d[:, 0:32 * T],
                                                          in_=ln1t[:, :]), cW)
                v_pbf_p1 = pbf_free[1]

            with contextlib.ExitStack() as esB:
                sbB = lambda n, s, d: esB.enter_context(nc.sbuf_tensor(n, s, d))
                mqT = sbB("mqT", [128, 4096], BF16)
                mkT = sbB("mkT", [128, 4096], BF16)
                mq = sbB("mq", [128, 4096], BF16)
                mk = sbB("mk", [128, 4096], BF16)
                mv = sbB("mv", [128, 9 * 128], BF16)
                expt = sbB("expt", [128, 1024], BF16)
                ogt = sbB("ogt", [128, 1024], F32)
                ogtok = sbB("ogtok", [128, 1024], F32)
                lno = sbB("lno", [128, 1024], F32)
                rden = sbB("rden", [128, 16], F32)

                P.w("gpsimd", cS, v_qkv_done)
                if PHASES >= 3 or MVTEST:
                    P.do("gpsimd", lambda g: g.memset(mv[:, 8::9], 1.0))
                v_mvones = P.nopinc("gpsimd", cW)

                v_tile_t = 0
                pbf_free3 = (cV, v_pbf_p1)
                psc_free = [(cS, v_qkv_done), (cS, v_qkv_done)]
                pmo_free = [(cS, v_qkv_done), (cS, v_qkv_done)]
                pog_free = [(cV, 0), (cV, 0)]
                expt_free = [(cT, 0), (cT, 0)]
                qt4 = qt.rearrange("p (g m u) -> p g m u", g=24, u=4)
                mqT4 = mqT.rearrange("p (m w g2) -> p m w g2", w=4, g2=32)
                mkT4 = mkT.rearrange("p (m w g2) -> p m w g2", w=4, g2=32)
                mv9 = mv.rearrange("p (s n) -> p s n", n=9)
                for t in range(min(NT3, NT) if PHASES >= 3 else 0):
                    P.w("vector", cS, v_qkv_done)
                    if v_tile_t:
                        P.w("vector", cT, v_tile_t)
                    for g in range(8):
                        src = qt4[:, g, t * 32:(t + 1) * 32, :]
                        dst = mqT4[:, :, :, g]
                        P.do("vector", lambda v, o=dst, i=src: v.tensor_copy(o, i))
                    for g in range(8):
                        src = qt4[:, 8 + g, t * 32:(t + 1) * 32, :]
                        dst = mkT4[:, :, :, g]
                        P.do("vector", lambda v, o=dst, i=src: v.tensor_copy(o, i))
                    if t == 0:
                        P.w("vector", cW, v_mvones)
                    for g in range(8):
                        src = qt[:, 16 * T + g * T + t * 128:
                                 16 * T + g * T + (t + 1) * 128]
                        dst = mv9[:, :, g]
                        P.do("vector", lambda v, o=dst, i=src: v.tensor_copy(o, i))
                    v_bld = P.nopinc("vector", cV)
                    v_mqmk = 0
                    for srcT, dstM in ((mqT, mq), (mkT, mk)):
                        for b in range(4):
                            P.w("tensor", cV, v_bld)
                            P.w("tensor", *pbf_free3)
                            v_tp = 0
                            for j in range(8):
                                blk = b * 8 + j
                                fn = (lambda tt, o=pbf[:, j * 128:(j + 1) * 128],
                                      i=srcT[:, blk * 128:(blk + 1) * 128],
                                      d=ident[:]: tt.transpose(o, i, d))
                                if j < 7:
                                    P.do("tensor", fn)
                                else:
                                    v_tp = P.inc("tensor", fn, cT)
                            P.w("vector", cT, v_tp)
                            v_ev = P.inc("vector",
                                         lambda v, o=dstM[:, b * 1024:(b + 1) * 1024],
                                         i=pbf[:, 0:1024]: v.tensor_copy(o, i), cV)
                            pbf_free3 = (cV, v_ev)
                            v_mqmk = v_ev
                    v_ogt = 0
                    v_om = 0
                    sbanks = ((pb[0], pb[1]), (pb[5], pb[7]))
                    for m_i in range(32 if SUB3 >= 2 else 0):
                        sl = m_i % 2
                        P.w("tensor", cV, v_mqmk)
                        v_scp = [0, 0]
                        for pr in range(2):
                            P.w("tensor", *psc_free[pr])
                            for w in range(2):
                                u = 2 * pr + w
                                fn = (lambda tt, o=sbanks[pr][w][:, 0:128],
                                      a=mk[32 * u:32 * u + 8,
                                           m_i * 128:(m_i + 1) * 128],
                                      b=mq[32 * u:32 * u + 8,
                                           m_i * 128:(m_i + 1) * 128],
                                      tp=(32 * u, 0):
                                      tt.matmul(o, a, b, start=True, stop=True,
                                                tile_position=tp))
                                if w < 1:
                                    P.do("tensor", fn)
                                else:
                                    v_scp[pr] = P.inc("tensor", fn, cT)
                        if SUB3 < 3:
                            for pr in range(2):
                                psc_free[pr] = (cS, P.nopinc("scalar", cS))
                            continue
                        P.w("scalar", *expt_free[sl])
                        v_exp = 0
                        for pr in range(2):
                            P.w("scalar", cT, v_scp[pr])
                            for w in range(2):
                                u = 2 * pr + w
                                fn = (lambda s,
                                      o=expt[:, sl * 512 + u * 128:
                                             sl * 512 + (u + 1) * 128],
                                      i=sbanks[pr][w][:, 0:128]:
                                      s.activation(o, i, AF.Exp))
                                if w < 1:
                                    P.do("scalar", fn)
                                else:
                                    v_exp = P.inc("scalar", fn, cS)
                            psc_free[pr] = (cS, v_exp)
                        if SUB3 < 4:
                            continue
                        mo = 2 + (m_i // 4) % 2
                        if m_i % 4 == 0:
                            P.w("tensor", *pmo_free[mo - 2])
                        P.w("tensor", cS, v_exp)
                        for u in range(4):
                            tok = 4 * m_i + u
                            fn = (lambda tt,
                                  o=pb[mo][:, (tok % 16) * 9:(tok % 16) * 9 + 9],
                                  a=expt[:, sl * 512 + u * 128:
                                         sl * 512 + (u + 1) * 128],
                                  b=mv[:, tok * 9:tok * 9 + 9]:
                                  tt.matmul(o, a, b, start=True, stop=True))
                            if u < 3:
                                P.do("tensor", fn)
                            else:
                                v_om = P.inc("tensor", fn, cT)
                        expt_free[sl] = (cT, v_om)
                        if SUB3 >= 5 and m_i % 4 == 3:
                            grp = m_i // 4
                            P.w("vector", cT, v_om)
                            mo9 = pb[mo][:, 0:144].rearrange("p (s n) -> p s n", n=9)
                            P.do("vector", lambda v, o=rden[:, 0:16], i=mo9[:, :, 8]:
                                 v.reciprocal(o, i))
                            P.do("vector", lambda v: v.engine_nop())
                            src1 = rden[:, 0:16].rearrange("p (s o) -> p s o", o=1) \
                                .broadcast_to([128, 16, 8])
                            dst = ogt.rearrange("p (g s) -> p s g", g=8)[
                                :, grp * 16:(grp + 1) * 16, :]
                            v_ev = P.inc("vector",
                                         lambda v, o=dst, a=mo9[:, :, 0:8], b=src1:
                                         v.tensor_tensor(o, a, b, ALU.mult), cV)
                            pmo_free[mo - 2] = (cV, v_ev)
                            v_ogt = v_ev
                    v_tile_t = v_om
                    v_ogtok = 0
                    for half in range(2 if SUB3 >= 6 else 0):
                        po = 4
                        P.w("tensor", cV, v_ogt)
                        P.w("tensor", *pog_free[0])
                        v_tp = 0
                        for j in range(4):
                            g = half * 4 + j
                            fn = (lambda tt, o=pb[po][:, j * 128:(j + 1) * 128],
                                  i=ogt[:, g * 128:(g + 1) * 128], d=identf[:]:
                                  tt.transpose(o, i, d))
                            if j < 3:
                                P.do("tensor", fn)
                            else:
                                v_tp = P.inc("tensor", fn, cT)
                        P.w("vector", cT, v_tp)
                        v_ev = P.inc("vector",
                                     lambda v, o=ogtok[:, half * 512:(half + 1) * 512],
                                     i=pb[po][:, 0:512]: v.tensor_copy(o, i), cV)
                        pog_free[0] = (cV, v_ev)
                        v_ogtok = v_ev
                    if SUB3 < 7:
                        continue
                    v_lno = emit_ln(ogtok[:, 0:1024], 1024, lnt[0][:, 0:1024],
                                    lno[:, 0:1024], 32 + 16 * (t % 2), cV, v_ogtok)
                    if SUB3 < 8:
                        continue
                    sl = t % 2
                    P.w("sync", *xb_free[sl])
                    v_xr = P.inc("sync", lambda sy, o=xb[sl][:],
                                 i=x_d[t * 128:(t + 1) * 128, :]:
                                 sy.dma_start(out=o, in_=i), cD)
                    P.w("vector", cD, v_xr)
                    P.w("vector", cS, v_lno)
                    xb4 = xb[sl].rearrange("p (d g w) -> p g d w", g=8, w=4)
                    lap = lno.rearrange("p (g d) -> p g d", g=8)
                    v_x2 = 0
                    for j in range(4):
                        fn = (lambda v, o=xb4[:, :, :, j], b=lap:
                              v.tensor_tensor(o, o, b, ALU.add))
                        if j < 3:
                            P.do("vector", fn)
                        else:
                            v_x2 = P.inc("vector", fn, cV)
                    P.w("sync", cV, v_x2)
                    v_sp = P.inc("sync",
                                 lambda sy, o=x2s_d[t * 128:(t + 1) * 128, :],
                                 i=xb[sl][:]: sy.dma_start(out=o, in_=i), cD)
                    v_last_spill[t] = v_sp
                    xb_free[sl] = (cD, v_sp)
                if DBG == 5:
                    for i, tt_ in enumerate([0, 3, 7]):
                        P.w("sync", cD, v_last_spill[tt_])
                        P.inc("sync", lambda sy, o=dbg_d[:, i * 4096:(i + 1) * 4096],
                              i2=x2s_d[tt_ * 128:(tt_ + 1) * 128, :]:
                              sy.dma_start(out=o, in_=i2), cD)
                v_p3_t = cT.v
                v_p3_v = cV.v
                v_p3_s = cS.v
                v_pbf_p3 = pbf_free3[1]

        with contextlib.ExitStack() as esC:
            sbC = lambda n, s, d: esC.enter_context(nc.sbuf_tensor(n, s, d))
            ln3t = sbC("ln3t", [128, 32 * 256], BF16)
            wg = [sbC(f"wg{i}", [128, 4096], BF16) for i in range(2)]
            wu = [sbC(f"wu{i}", [128, 4096], BF16) for i in range(2)]
            wd = [sbC(f"wd{i}", [128, 2048], BF16) for i in range(2)]
            hid = sbC("hid", [128, NU * 256], BF16)
            gsb = [sbC(f"gsb{i}", [128, 256], BF16) for i in range(2)]
            usb = [sbC(f"usb{i}", [128, 256], BF16) for i in range(2)]
            ysd = [sbC(f"ysd{i}", [128, D], F32) for i in range(2)]
            sdb = sbC("sdb", [128, D], BF16)
            sdr = sbC("sdr", [1, D + 128], BF16)

            for e in ENGS:
                P.w(e, cT, v_p3_t)
                P.w(e, cV, v_p3_v)
                P.w(e, cS, v_p3_s)
            if PHASES >= 4 or SDTEST:
                P.inc("gpsimd", lambda g: g.dma_start(out=sdr[0:1, 0:D], in_=sd_d[:, :]), cW)
                P.do("gpsimd", lambda g: g.memset(sdr[0:1, D:D + 128], 1.0))
                v_sdr = P.nopinc("gpsimd", cW)
                P.w("tensor", cW, v_sdr)
            v_sdb = 0
            for fg in range(8 if (PHASES >= 4 or SDTEST) else 0):
                if fg > 0:
                    P.w("tensor", cS, v_sdb)
                fn = (lambda tt, o=pb[7][:, 0:512], a=sdr[0:1, D:D + 128],
                      b=sdr[0:1, fg * 512:(fg + 1) * 512]:
                      tt.matmul(o, a, b, start=True, stop=True))
                v_bc = P.inc("tensor", fn, cT)
                P.w("scalar", cT, v_bc)
                v_sdb = P.inc("scalar",
                              lambda s, o=sdb[:, fg * 512:(fg + 1) * 512],
                              i=pb[7][:, 0:512]:
                              s.activation(o, i, AF.Identity, scale=1.0), cS)

            pbf_free4 = (cV, v_pbf_p3)
            lnt_free4 = [(cV, 0), (cV, 0)]
            wgu_free = [(cT, 0), (cT, 0)]
            wd_free = [(cT, 0), (cT, 0)]
            pgu_free = [(cS, v_sdb), (cS, v_sdb)]
            gu_free = [(cV, 0), (cV, 0)]
            ysd_free = [(cD, 0), (cD, 0)]
            for q in range(NQ if PHASES >= 4 else 0):
                v_ln3 = 0
                for tt2 in range(2):
                    t = 2 * q + tt2
                    sl = t % 2
                    P.w("sync", *xb_free[sl])
                    P.w("sync", cD, v_last_spill[t])
                    v_xr = P.inc("sync", lambda sy, o=xb[sl][:],
                                 i=x2s_d[t * 128:(t + 1) * 128, :]:
                                 sy.dma_start(out=o, in_=i), cD)
                    P.w("scalar", *lnt_free4[sl])
                    v_ap2 = emit_ln(xb[sl][:, 0:D], D, lnt[sl][:, 0:D],
                                    lnt[sl][:, 0:D], 64 + 16 * sl, cD, v_xr)
                    xb_free[sl] = (cS, v_ap2)
                    for b in range(4 if SUB >= 1 else 0):
                        P.w("tensor", cS, v_ap2)
                        P.w("tensor", *pbf_free4)
                        v_tp = 0
                        for j in range(8):
                            c = b * 8 + j
                            fn = (lambda tt3, o=pbf[:, j * 128:(j + 1) * 128],
                                  i=lnt[sl][:, c * 128:(c + 1) * 128], d=ident[:]:
                                  tt3.transpose(o, i, d))
                            if j < 7:
                                P.do("tensor", fn)
                            else:
                                v_tp = P.inc("tensor", fn, cT)
                        P.w("vector", cT, v_tp)
                        dst = ln3t.rearrange("p (c s) -> p c s", s=256)[
                            :, b * 8:(b + 1) * 8, tt2 * 128:(tt2 + 1) * 128]
                        src = pbf.rearrange("p (c i) -> p c i", i=128)
                        v_ev = P.inc("vector", lambda v, o=dst, i=src:
                                     v.tensor_copy(o, i), cV)
                        pbf_free4 = (cV, v_ev)
                        if b == 3:
                            lnt_free4[sl] = (cV, v_ev)
                            v_ln3 = v_ev
                v_hid = 0
                for u in range(NU):
                    h = u % 2
                    P.w("gpsimd", *wgu_free[h])
                    P.inc("gpsimd", lambda g, o=wg[h][:], i=wgb_d[u, :, :, :]:
                          g.dma_start(out=o, in_=i), cW)
                    v_wu = P.inc("gpsimd", lambda g, o=wu[h][:], i=wub_d[u, :, :, :]:
                                 g.dma_start(out=o, in_=i), cW)
                    if u == 0:
                        P.w("tensor", cV, v_ln3)
                    P.w("tensor", cW, v_wu)
                    P.w("tensor", *pgu_free[h])
                    for c in range(32):
                        P.do("tensor",
                             (lambda tt3, o=pb[h][:, 0:256],
                              a=wg[h][:, c * 128:(c + 1) * 128],
                              b=ln3t[:, c * 256:(c + 1) * 256], s0=(c == 0):
                              tt3.matmul(o, a, b, start=s0, stop=(c == 31))))
                    v_gu = 0
                    for c in range(32):
                        fn = (lambda tt3, o=pb[2 + h][:, 0:256],
                              a=wu[h][:, c * 128:(c + 1) * 128],
                              b=ln3t[:, c * 256:(c + 1) * 256], s0=(c == 0):
                              tt3.matmul(o, a, b, start=s0, stop=(c == 31)))
                        if c < 31:
                            P.do("tensor", fn)
                        else:
                            v_gu = P.inc("tensor", fn, cT)
                    wgu_free[h] = (cT, v_gu)
                    P.w("scalar", cT, v_gu)
                    P.w("scalar", *gu_free[h])
                    P.do("scalar", lambda s, o=gsb[h][:], i=pb[h][:, 0:256],
                         sc=sg_s[:, u:u + 1]: s.activation(o, i, AF.Silu, scale=sc))
                    v_us = P.inc("scalar", lambda s, o=usb[h][:], i=pb[2 + h][:, 0:256],
                                 sc=su_s[:, u:u + 1]:
                                 s.activation(o, i, AF.Identity, scale=sc), cS)
                    P.w("vector", cS, v_us)
                    v_hid = P.inc("vector",
                                  lambda v, o=hid[:, u * 256:(u + 1) * 256],
                                  a=gsb[h][:], b=usb[h][:]:
                                  v.tensor_tensor(o, a, b, ALU.mult), cV)
                    pgu_free[h] = (cV, v_hid)
                    gu_free[h] = (cV, v_hid)
                v_ev_fh = 0
                v_dn = 0
                for fh in range(2):
                    for u in range(NU):
                        h = u % 2
                        P.w("gpsimd", *wd_free[h])
                        v_wd = P.inc("gpsimd", lambda g, o=wd[h][:],
                                     i=wdb_d[u, :, fh * 2048:(fh + 1) * 2048]:
                                     g.dma_start(out=o, in_=i), cW)
                        P.w("tensor", cW, v_wd)
                        if u == 0:
                            P.w("tensor", cV, v_hid)
                            if fh == 1:
                                P.w("tensor", cV, v_ev_fh)
                            else:
                                P.w("tensor", *ysd_free[0])
                                P.w("tensor", *ysd_free[1])
                                if q == 0:
                                    P.w("tensor", cS, v_sdb)
                        for st in range(2):
                            for fg in range(4):
                                fn = (lambda tt3, o=pb[st * 4 + fg][:, 0:512],
                                      a=hid[:, u * 256 + st * 128:
                                            u * 256 + (st + 1) * 128],
                                      b=wd[h][:, fg * 512:(fg + 1) * 512],
                                      s0=(u == 0), sp=(u == NU - 1):
                                      tt3.matmul(o, a, b, start=s0, stop=sp))
                                if st == 1 and fg == 3:
                                    v_dn = P.inc("tensor", fn, cT)
                                else:
                                    P.do("tensor", fn)
                        wd_free[h] = (cT, v_dn)
                    P.w("vector", cT, v_dn)
                    for st in range(2):
                        for fg in range(4):
                            fn = (lambda v,
                                  o=ysd[st][:, fh * 2048 + fg * 512:
                                            fh * 2048 + (fg + 1) * 512],
                                  a=pb[st * 4 + fg][:, 0:512],
                                  b=sdb[:, fh * 2048 + fg * 512:
                                        fh * 2048 + (fg + 1) * 512]:
                                  v.tensor_tensor(o, a, b, ALU.mult))
                            if st == 1 and fg == 3:
                                v_ev_fh = P.inc("vector", fn, cV)
                            else:
                                P.do("vector", fn)
                pbf_free4 = (cV, v_ev_fh)
                pgu_free = [(cV, v_ev_fh), (cV, v_ev_fh)]
                for st in range(2):
                    t128 = q * 256 + st * 128
                    v_lny = emit_ln(ysd[st][:, 0:D], D, lnt[0][:, 0:D],
                                    ysd[st][:, 0:D], 96 + 16 * st, cV, v_ev_fh)
                    sl = st
                    P.w("sync", *xb_free[sl])
                    v_xr = P.inc("sync", lambda sy, o=xb[sl][:],
                                 i=x2s_d[t128:t128 + 128, :]:
                                 sy.dma_start(out=o, in_=i), cD)
                    P.w("vector", cD, v_xr)
                    P.w("vector", cS, v_lny)
                    v_add = P.inc("vector", lambda v, o=ysd[st][:], b=xb[sl][:]:
                                  v.tensor_tensor(o, o, b, ALU.add), cV)
                    xb_free[sl] = (cV, v_add)
                    P.w("sync", cV, v_add)
                    v_of = P.inc("sync", lambda sy, o=out_d[t128:t128 + 128, :],
                                 i=ysd[st][:]: sy.dma_start(out=o, in_=i), cD)
                    ysd_free[st] = (cD, v_of)
                lnt_free4[0] = (cS, cS.v)
                lnt_free4[1] = (cS, cS.v)

        P.w("sync", cD, cD.v)
        P.w("gpsimd", cW, cW.v)

        import os
        if os.environ.get("KQ_PRINT"):
            print("instr counts:", {e: len(P.q[e]) for e in ENGS})
        with nc.Block() as block:
            block.sync(lambda E: [f(E) for f in P.q["sync"]])
            block.gpsimd(lambda E: [f(E) for f in P.q["gpsimd"]])
            block.tensor(lambda E: [f(E) for f in P.q["tensor"]])
            block.vector(lambda E: [f(E) for f in P.q["vector"]])
            block.scalar(lambda E: [f(E) for f in P.q["scalar"]])
    return nc


_NC_CACHE = None
_DBG_OUT = None


def _host_layout(inputs):
    c = np.ascontiguousarray
    wq = np.asarray(inputs["L0_attn_Wq"], np.float32)
    wqb = c(wq.reshape(32, 128, 128, 8, 4).transpose(3, 1, 0, 4, 2))
    bq = c(np.asarray(inputs["L0_attn_bq"], np.float32).reshape(1, 4096))
    wk = np.asarray(inputs["L0_attn_Wk"], np.float32)
    wkb = c(wk.reshape(32, 128, 128, 8).transpose(3, 1, 0, 2))
    bk = c(np.asarray(inputs["L0_attn_bk"], np.float32).reshape(1, 1024))
    wv = np.asarray(inputs["L0_attn_Wv"], np.float32)
    wvb = c(wv.reshape(32, 128, 128, 8).transpose(3, 1, 0, 2))
    bv = c(np.asarray(inputs["L0_attn_bv"], np.float32).reshape(1, 1024))
    wgt = np.asarray(inputs["L0_ffn_w_gate"], np.int8)
    wgb = c(wgt.reshape(NU, 128, 32, 128).transpose(0, 3, 2, 1))
    wut = np.asarray(inputs["L0_ffn_w_up"], np.int8)
    wub = c(wut.reshape(NU, 128, 32, 128).transpose(0, 3, 2, 1))
    wdt = np.asarray(inputs["L0_ffn_w_down"], np.int8)
    wdb = c(wdt.T.reshape(NU, 128, 4096))
    sg = c(np.asarray(inputs["L0_ffn_s_gate"], np.float32).reshape(NU, 128).T)
    su = c(np.asarray(inputs["L0_ffn_s_up"], np.float32).reshape(NU, 128).T)
    sd = c(np.asarray(inputs["L0_ffn_s_down"], np.float32).reshape(1, 4096))
    return dict(wqb=wqb, bq=bq, wkb=wkb, bk=bk, wvb=wvb, bv=bv,
                wgb=wgb, wub=wub, wdb=wdb, sg=sg, su=su, sd=sd)


def kernel(**inputs):
    global _NC_CACHE
    x = np.ascontiguousarray(np.asarray(inputs["L0_attn_x"], np.float32)
                             .reshape(8192, 4096))
    w = _host_layout(inputs)
    if _NC_CACHE is None:
        _NC_CACHE = build_nc()
    nc = _NC_CACHE
    in_maps = []
    for ci in range(NCORES):
        m = dict(w)
        m["x"] = x[ci * T:(ci + 1) * T, :]
        in_maps.append(m)
    res = run_bass_kernel_spmd(nc, in_maps, core_ids=list(range(NCORES)))
    if DBG:
        global _DBG_OUT
        _DBG_OUT = [res.results[ci].get("dbg") for ci in range(NCORES)]
    out = np.concatenate([res.results[ci]["out"] for ci in range(NCORES)], axis=0)
    return out.reshape(1, 8192, 4096).astype(np.float32)


if __name__ == "__main__":
    import reference as _r
    ins = _r.setup_inputs(0)
    got = kernel(**{k: np.asarray(v) for k, v in ins.items()})
    want = np.asarray(_r.reference(**ins))
    err = got - want
    rel = float(np.sqrt((err ** 2).mean() / (want ** 2).mean()))
    print("rel_l2", rel, "max", float(np.abs(err).max()))
